# version 01 (final)
# baseline (speedup 1.0000x reference)
import numpy as np
import ml_dtypes
from contextlib import ExitStack

import concourse.bass as bass
import concourse.mybir as mybir
from concourse.bass_utils import run_bass_kernel_spmd

F32 = mybir.dt.float32
BF16 = mybir.dt.bfloat16
AF = mybir.ActivationFunctionType
ALU = mybir.AluOpType
AX = mybir.AxisListType

NCORES = 8
B = 2
S = 8192
D = 1024
TOK = 2048
NB = D // 128
H = 16
HD = 64
HPC = 4
CS = 128

B0_A0 = 16.0
B0_QKN = 8.0

BF = ml_dtypes.bfloat16


class St:
    def __init__(self):
        self.ops = []
        self.total = 0
        self.stage_start = 0

    def barrier(self):
        self.stage_start = self.total

    def emit(self, eng, fn, inc=1):
        self.ops.append((eng, self.stage_start, fn, inc))
        self.total += inc

    def dma(self, eng, out, in_, **kw):
        self.emit(eng, lambda e, o=out, i=in_, k=kw: e.dma_start(out=o, in_=i, **k), 16)

    def build(self, block, sem):
        per = {}
        for eng, wait, fn, inc in self.ops:
            per.setdefault(eng, []).append((wait, fn, inc))
        for name, lst in per.items():
            def mk(lst):
                def body(e):
                    last = 0
                    for wait, fn, inc in lst:
                        if wait > last:
                            e.wait_ge(sem, wait)
                            last = wait
                        fn(e).then_inc(sem, inc)
                return body
            getattr(block, name)(mk(lst))

        def fin(e):
            e.wait_ge(sem, self.total)
        block.sync(fin)


def pipeN(s, slots_list):
    if not slots_list:
        return
    nslots = max(len(x) for x in slots_list)
    n = len(slots_list)
    for stage in range(-1, n + nslots - 1):
        s.barrier()
        for i in range(max(0, stage - nslots + 2), min(n, stage + 2)):
            j = stage - i + 1
            if 0 <= j < len(slots_list[i]):
                for f in slots_list[i][j]:
                    f(s)
    s.barrier()


def make_block(nc, es):
    sem = es.enter_context(nc.semaphore("sem"))
    block = es.enter_context(nc.Block())
    return sem, block




def const_col(s, es, nc, name, val):
    col = es.enter_context(nc.sbuf_tensor(name, [128, 1], F32))
    s.barrier()
    s.emit("vector", lambda e: e.memset(col[:], float(val)))
    s.barrier()
    return col

def dma_w(s, buf_ap, w_slice, nib, ncols):
    s.dma("gpsimd", buf_ap.rearrange("p (ib o) -> p ib o", ib=nib),
          w_slice.rearrange("(ib p) o -> p ib o", p=128))


def emit_ln(s, es, nc, x_dram, xhat, nblk, tokn, eps=1e-5, sub_mean=True,
            gamma_col=None, prefix="ln", out32_dram=None):
    ncn = tokn // 512
    les = ExitStack()
    xb = [les.enter_context(nc.sbuf_tensor(f"{prefix}_xb{i}", [128, tokn], BF16))
          for i in range(nblk)]
    ones_bc = les.enter_context(nc.sbuf_tensor(f"{prefix}_ones", [128, 128], BF16))
    rb = les.enter_context(nc.sbuf_tensor(f"{prefix}_rb", [128, tokn], F32))
    e1 = les.enter_context(nc.sbuf_tensor(f"{prefix}_e1", [128, tokn], F32))
    e2 = les.enter_context(nc.sbuf_tensor(f"{prefix}_e2", [128, tokn], F32))
    sqs = [les.enter_context(nc.sbuf_tensor(f"{prefix}_sq{j}", [128, tokn], BF16))
           for j in range(2)]
    nmb = None
    if sub_mean:
        nmb = les.enter_context(nc.sbuf_tensor(f"{prefix}_nmb", [128, tokn], F32))
    o32 = None
    if out32_dram is not None:
        o32 = [les.enter_context(nc.sbuf_tensor(f"{prefix}_o32{j}", [128, tokn], F32))
               for j in range(2)]

    epsc = const_col(s, les, nc, f"{prefix}_eps", eps)
    with nc.psum_tensor(f"{prefix}_pm1", [128, tokn], F32) as pm1, \
         nc.psum_tensor(f"{prefix}_pm2", [128, tokn], F32) as pm2:
        s.barrier()
        for i in range(nblk):
            s.dma("gpsimd", xb[i][:], x_dram[i * 128:(i + 1) * 128, :])
        s.emit("vector", lambda e: e.memset(ones_bc[:], 1.0))
        slots = []
        for i in range(nblk):
            sq_ops = [lambda s, i=i: s.emit("scalar", lambda e: e.activation(
                sqs[i % 2][:], xb[i][:], AF.Square))]
            mm_ops = []
            for c in range(ncn):
                if sub_mean:
                    mm_ops.append(lambda s, i=i, c=c: s.emit("tensor", lambda e: e.matmul(
                        pm1[:, c * 512:(c + 1) * 512], ones_bc[:],
                        xb[i][:, c * 512:(c + 1) * 512],
                        start=(i == 0), stop=(i == nblk - 1))))
                mm_ops.append(lambda s, i=i, c=c: s.emit("tensor", lambda e: e.matmul(
                    pm2[:, c * 512:(c + 1) * 512], ones_bc[:],
                    sqs[i % 2][:, c * 512:(c + 1) * 512],
                    start=(i == 0), stop=(i == nblk - 1))))
            slots.append([sq_ops, mm_ops])
        pipeN(s, slots)

        nd = float(nblk * 128)
        s.barrier()
        if sub_mean:
            s.emit("vector", lambda e: e.tensor_copy(e1[:], pm1[:]))
        s.emit("scalar", lambda e: e.copy(e2[:], pm2[:]))
        s.barrier()
        if sub_mean:
            s.emit("vector", lambda e: e.scalar_tensor_tensor(
                nmb[:], e1[:], 1.0 / (nd * nd), e1[:], ALU.mult, ALU.mult))
            s.barrier()
            s.emit("vector", lambda e: e.scalar_tensor_tensor(
                e2[:], e2[:], 1.0 / nd, nmb[:], ALU.mult, ALU.subtract))
        else:
            s.emit("vector", lambda e: e.tensor_scalar(
                e2[:], e2[:], 1.0 / nd, None, ALU.mult))
        s.barrier()
        s.emit("scalar", lambda e: e.activation(e2[:], e2[:], AF.Ln, bias=epsc[:]))
        s.barrier()
        s.emit("scalar", lambda e: e.activation(rb[:], e2[:], AF.Exp, scale=-0.5))
        s.barrier()
        if sub_mean:
            s.emit("vector", lambda e: e.scalar_tensor_tensor(
                nmb[:], e1[:], -1.0 / nd, rb[:], ALU.mult, ALU.mult))
            s.barrier()
        for i in range(nblk):
            if sub_mean:
                s.emit("vector", lambda e, i=i: e.tensor_mul(e1[:], xb[i][:], rb[:]))
                s.barrier()
                if o32 is not None:
                    s.emit("vector", lambda e, i=i: e.tensor_add(o32[i % 2][:], e1[:], nmb[:]))
                    s.barrier()
                    s.emit("vector", lambda e, i=i: e.tensor_copy(xhat[i][:], o32[i % 2][:]))
                    s.dma("sync", out32_dram[i * 128:(i + 1) * 128, :], o32[i % 2][:])
                else:
                    s.emit("vector", lambda e, i=i: e.tensor_add(xhat[i][:], e1[:], nmb[:]))
                s.barrier()
            elif gamma_col is not None:
                s.emit("vector", lambda e, i=i: e.tensor_mul(e1[:], xb[i][:], rb[:]))
                s.barrier()
                s.emit("vector", lambda e, i=i: e.tensor_scalar(
                    xhat[i][:], e1[:], gamma_col[:, i:i + 1], None, ALU.mult))
                s.barrier()
            else:
                s.emit("vector", lambda e, i=i: e.tensor_mul(xhat[i][:], xb[i][:], rb[:]))
                s.barrier()
    s.barrier()
    les.close()


def emit_proj_cm(s, es, nc, xhat, w_dram, out_dram, n_in_blk, n_out_blk, tokn,
                 prefix, x_dram=None, out_f32=False, sb_out=None,
                 out2_dram=None, x2_dram=None, epilogue=None, ev_override=None):
    ncn = tokn // 512
    les = ExitStack()
    wbuf = [les.enter_context(nc.sbuf_tensor(f"{prefix}_w{j}", [128, n_in_blk * 128], BF16))
            for j in range(2)]
    dt = F32 if (out_f32 or x_dram is not None) else BF16
    obuf = [les.enter_context(nc.sbuf_tensor(f"{prefix}_o{j}", [128, tokn], dt))
            for j in range(3)]
    xres = None
    if x_dram is not None:
        xres = [les.enter_context(nc.sbuf_tensor(f"{prefix}_xr{j}", [128, tokn], F32))
                for j in range(2)]
    o2 = x2 = None
    if out2_dram is not None:
        o2 = [les.enter_context(nc.sbuf_tensor(f"{prefix}_o2{j}", [128, tokn], F32))
              for j in range(2)]
        x2 = [les.enter_context(nc.sbuf_tensor(f"{prefix}_x2{j}", [128, tokn], F32))
              for j in range(2)]
    psum = [les.enter_context(nc.psum_tensor(f"{prefix}_p{j}", [128, tokn], F32))
            for j in range(2)]

    slots = []
    for ot in range(n_out_blk):
        ld = [lambda s, ot=ot: dma_w(
            s, wbuf[ot % 2][:],
            w_dram[:n_in_blk * 128, ot * 128:(ot + 1) * 128], n_in_blk, 128)]
        mm = []
        if xres is not None:
            mm.append(lambda s, ot=ot: s.dma(
                "sync", xres[ot % 2][:], x_dram[ot * 128:(ot + 1) * 128, :]))
        for c in range(ncn):
            for ib in range(n_in_blk):
                mm.append(lambda s, ot=ot, c=c, ib=ib: s.emit("tensor", lambda e: e.matmul(
                    psum[ot % 2][:, c * 512:(c + 1) * 512],
                    wbuf[ot % 2][:, ib * 128:(ib + 1) * 128],
                    xhat[ib][:, c * 512:(c + 1) * 512],
                    start=(ib == 0), stop=(ib == n_in_blk - 1))))
        ev = []
        if x2 is not None:
            ev.append(lambda s, ot=ot: s.dma(
                "sync", x2[ot % 2][:], x2_dram[ot * 128:(ot + 1) * 128, :]))
        if ev_override is not None:
            ev.extend(ev_override(ot, psum[ot % 2], obuf[ot % 3]))
        elif xres is not None:
            ev.append(lambda s, ot=ot: s.emit("vector", lambda e: e.tensor_add(
                obuf[ot % 3][:], psum[ot % 2][:], xres[ot % 2][:])))
        else:
            ev.append(lambda s, ot=ot: s.emit("vector", lambda e: e.tensor_copy(
                obuf[ot % 3][:], psum[ot % 2][:])))
        extra = []
        if o2 is not None:
            extra.append(lambda s, ot=ot: s.emit("vector", lambda e: e.tensor_add(
                o2[ot % 2][:], obuf[ot % 3][:], x2[ot % 2][:])))
        if sb_out is not None:
            extra.append(lambda s, ot=ot: s.emit("vector", lambda e: e.tensor_copy(
                sb_out[ot][:], obuf[ot % 3][:])))
        if epilogue is not None:
            extra.extend(epilogue(ot, obuf[ot % 3]))
        st = []
        if out_dram is not None:
            st.append(lambda s, ot=ot: s.dma(
                "sync", out_dram[ot * 128:(ot + 1) * 128, :], obuf[ot % 3][:]))
        if out2_dram is not None:
            st.append(lambda s, ot=ot: s.dma(
                "sync", out2_dram[ot * 128:(ot + 1) * 128, :], o2[ot % 2][:]))
        slots.append([ld, mm, ev, extra, st])
    pipeN(s, slots)
    les.close()


def load_w_tm(s, es, nc, w_dram, n_in_blk, n_cols, prefix):
    wsb = es.enter_context(nc.sbuf_tensor(f"{prefix}_w", [128, n_in_blk * n_cols], BF16))
    s.barrier()
    dma_w(s, wsb[:], w_dram[:n_in_blk * 128, :n_cols], n_in_blk, n_cols)
    s.barrier()
    return wsb


def emit_proj_tm(s, es, nc, xhat, w_sb, n_in_blk, n_cols, tokn, prefix,
                 epilogue, psum_w=None):
    ntt = tokn // 128
    les = ExitStack()
    pw = psum_w or max(512, n_cols)
    psum = [les.enter_context(nc.psum_tensor(f"{prefix}_p{j}", [128, pw], F32))
            for j in range(3)]
    slots = []
    ncc = (n_cols + 511) // 512
    for tt in range(ntt):
        mm = []
        for cc in range(ncc):
            c0 = cc * 512
            c1 = min(n_cols, c0 + 512)
            for ib in range(n_in_blk):
                mm.append(lambda s, tt=tt, ib=ib, c0=c0, c1=c1: s.emit(
                    "tensor", lambda e: e.matmul(
                        psum[tt % 3][:, c0:c1],
                        xhat[ib][:, tt * 128:(tt + 1) * 128],
                        w_sb[:, ib * n_cols + c0:ib * n_cols + c1],
                        start=(ib == 0), stop=(ib == n_in_blk - 1))))
        slots.append([[], mm] + epilogue(tt, psum[tt % 3]))
    pipeN(s, slots)
    les.close()



def build_tok_pre(n_out_blk):
    nc = bass.Bass("TRN2", target_bir_lowering=False)
    x = nc.dram_tensor("x", [D, TOK], F32, kind="ExternalInput").ap()
    w = nc.dram_tensor("w", [D, n_out_blk * 128], F32, kind="ExternalInput").ap()
    out = nc.dram_tensor("out", [n_out_blk * 128, TOK], BF16, kind="ExternalOutput").ap()
    es = ExitStack()
    xhat = [es.enter_context(nc.sbuf_tensor(f"xhat{i}", [128, TOK], BF16)) for i in range(NB)]
    sem, block = make_block(nc, es)
    s = St()
    emit_ln(s, es, nc, x, xhat, NB, TOK)
    emit_proj_cm(s, es, nc, xhat, w, out, NB, n_out_blk, TOK, "pr")
    s.build(block, sem)
    es.close()
    return nc


def build_tok_post():
    nc = bass.Bass("TRN2", target_bir_lowering=False)
    o = nc.dram_tensor("o", [D, TOK], BF16, kind="ExternalInput").ap()
    w = nc.dram_tensor("w", [D, D], F32, kind="ExternalInput").ap()
    x = nc.dram_tensor("x", [D, TOK], F32, kind="ExternalInput").ap()
    xn = nc.dram_tensor("xn", [D, TOK], F32, kind="ExternalOutput").ap()
    es = ExitStack()
    ohat = [es.enter_context(nc.sbuf_tensor(f"oh{i}", [128, TOK], BF16)) for i in range(NB)]
    sem, block = make_block(nc, es)
    s = St()
    s.barrier()
    for i in range(NB):
        s.dma("gpsimd", ohat[i][:], o[i * 128:(i + 1) * 128, :])
    s.barrier()
    emit_proj_cm(s, es, nc, ohat, w, xn, NB, NB, TOK, "po", x_dram=x, out_f32=True)
    s.build(block, sem)
    es.close()
    return nc


def build_ffn():
    nc = bass.Bass("TRN2", target_bir_lowering=False)
    x = nc.dram_tensor("x", [D, TOK], F32, kind="ExternalInput").ap()
    w1t = nc.dram_tensor("w1t", [D, 4096], F32, kind="ExternalInput").ap()
    w2t = nc.dram_tensor("w2t", [D, 4096], F32, kind="ExternalInput").ap()
    h1t = nc.dram_tensor("h1t", [4096, 4096], F32, kind="ExternalInput").ap()
    h2t = nc.dram_tensor("h2t", [4096, 4096], F32, kind="ExternalInput").ap()
    w3t = nc.dram_tensor("w3t", [4096, D], F32, kind="ExternalInput").ap()
    s1 = nc.dram_tensor("s1", [128, 32], F32, kind="ExternalInput").ap()
    s2 = nc.dram_tensor("s2", [128, 32], F32, kind="ExternalInput").ap()
    sh1 = nc.dram_tensor("sh1", [128, 32], F32, kind="ExternalInput").ap()
    sh2 = nc.dram_tensor("sh2", [128, 32], F32, kind="ExternalInput").ap()
    s3 = nc.dram_tensor("s3", [128, 8], F32, kind="ExternalInput").ap()
    xn = nc.dram_tensor("xn", [D, TOK], F32, kind="ExternalOutput").ap()

    HB = 32
    SLAB = 1024
    es = ExitStack()
    xhat = [es.enter_context(nc.sbuf_tensor(f"xhat{i}", [128, TOK], BF16)) for i in range(NB)]
    s1c = es.enter_context(nc.sbuf_tensor("s1c", [128, HB], F32))
    s2c = es.enter_context(nc.sbuf_tensor("s2c", [128, HB], F32))
    sh1c = es.enter_context(nc.sbuf_tensor("sh1c", [128, HB], F32))
    sh2c = es.enter_context(nc.sbuf_tensor("sh2c", [128, HB], F32))
    s3c = es.enter_context(nc.sbuf_tensor("s3c", [128, NB], F32))

    sem, block = make_block(nc, es)
    s = St()
    s.barrier()
    s.dma("sync", s1c[:], s1)
    s.dma("sync", s2c[:], s2)
    s.dma("sync", sh1c[:], sh1)
    s.dma("sync", sh2c[:], sh2)
    s.dma("sync", s3c[:], s3)
    emit_ln(s, es, nc, x, xhat, NB, TOK)
    ps = [es.enter_context(nc.psum_tensor(f"ps{j}", [128, 512], F32)) for j in range(6)]
    hbuf = [es.enter_context(nc.sbuf_tensor(f"h{i}", [128, SLAB], BF16)) for i in range(HB)]
    gbuf = [es.enter_context(nc.sbuf_tensor(f"g{i}", [128, SLAB], BF16)) for i in range(HB)]
    silu_t = es.enter_context(nc.sbuf_tensor("silu_t", [128, 512 * 3], BF16))
    w12buf = [es.enter_context(nc.sbuf_tensor(f"w12_{j}", [128, 2 * NB * 128], BF16))
              for j in range(2)]
    whbuf = [es.enter_context(nc.sbuf_tensor(f"wh_{j}", [128, HB * 128], BF16))
             for j in range(2)]
    xresb = [es.enter_context(nc.sbuf_tensor(f"xr_{j}", [128, SLAB], F32)) for j in range(2)]
    obuf = [es.enter_context(nc.sbuf_tensor(f"ob_{j}", [128, SLAB], F32)) for j in range(2)]

    for slab in range(TOK // SLAB):
        t0 = slab * SLAB
        slots = []
        ti = 0
        for hb in range(HB):
            for c in range(SLAB // 512):
                ld = []
                if c == 0:
                    ld = [lambda s, hb=hb: dma_w(
                        s, w12buf[hb % 2][:, :NB * 128],
                        w1t[:, hb * 128:(hb + 1) * 128], NB, 128),
                        lambda s, hb=hb: dma_w(
                        s, w12buf[hb % 2][:, NB * 128:],
                        w2t[:, hb * 128:(hb + 1) * 128], NB, 128)]
                p1 = ps[2 * (ti % 3)]
                p2 = ps[2 * (ti % 3) + 1]
                mm = []
                for ib in range(NB):
                    mm.append(lambda s, hb=hb, c=c, ib=ib, p1=p1, t0=t0: s.emit(
                        "tensor", lambda e, hb=hb, c=c, ib=ib, p1=p1, t0=t0: e.matmul(
                            p1[:], w12buf[hb % 2][:, ib * 128:(ib + 1) * 128],
                            xhat[ib][:, t0 + c * 512:t0 + (c + 1) * 512],
                            start=(ib == 0), stop=(ib == NB - 1))))
                for ib in range(NB):
                    mm.append(lambda s, hb=hb, c=c, ib=ib, p2=p2, t0=t0: s.emit(
                        "tensor", lambda e, hb=hb, c=c, ib=ib, p2=p2, t0=t0: e.matmul(
                            p2[:], w12buf[hb % 2][:, NB * 128 + ib * 128:NB * 128 + (ib + 1) * 128],
                            xhat[ib][:, t0 + c * 512:t0 + (c + 1) * 512],
                            start=(ib == 0), stop=(ib == NB - 1))))
                ev1 = [lambda s, hb=hb, ti=ti, p1=p1: s.emit("scalar", lambda e: e.activation(
                    silu_t[:, 512 * (ti % 3):512 * (ti % 3 + 1)], p1[:], AF.Silu,
                    scale=s1c[:, hb:hb + 1]))]
                ev2 = [lambda s, hb=hb, c=c, ti=ti, p2=p2: s.emit(
                    "vector", lambda e: e.scalar_tensor_tensor(
                        hbuf[hb][:, c * 512:(c + 1) * 512], p2[:], s2c[:, hb:hb + 1],
                        silu_t[:, 512 * (ti % 3):512 * (ti % 3 + 1)], ALU.mult, ALU.mult))]
                slots.append([ld, mm, ev1, ev2])
                ti += 1
        pipeN(s, slots)

        for (wt, scol, src, dst) in ((h1t, sh1c, hbuf, gbuf), (h2t, sh2c, gbuf, hbuf)):
            slots = []
            ti = 0
            for ob in range(HB):
                for c in range(SLAB // 512):
                    ld = []
                    if c == 0:
                        ld = [lambda s, ob=ob, wt=wt: dma_w(
                            s, whbuf[ob % 2][:],
                            wt[:, ob * 128:(ob + 1) * 128], HB, 128)]
                    p1 = ps[ti % 6]
                    mm = []
                    for ib in range(HB):
                        mm.append(lambda s, ob=ob, c=c, ib=ib, p1=p1, src=src: s.emit(
                            "tensor", lambda e: e.matmul(
                                p1[:], whbuf[ob % 2][:, ib * 128:(ib + 1) * 128],
                                src[ib][:, c * 512:(c + 1) * 512],
                                start=(ib == 0), stop=(ib == HB - 1))))
                    ev = [lambda s, ob=ob, c=c, p1=p1, scol=scol, dst=dst: s.emit(
                        "scalar", lambda e: e.activation(
                            dst[ob][:, c * 512:(c + 1) * 512], p1[:], AF.Gelu,
                            scale=scol[:, ob:ob + 1]))]
                    slots.append([ld, mm, ev])
                    ti += 1
            pipeN(s, slots)

        slots = []
        ti = 0
        for ob in range(NB):
            for c in range(SLAB // 512):
                ld = []
                if c == 0:
                    ld = [lambda s, ob=ob: dma_w(
                        s, whbuf[ob % 2][:],
                        w3t[:, ob * 128:(ob + 1) * 128], HB, 128),
                        lambda s, ob=ob, t0=t0: s.dma(
                        "sync", xresb[ob % 2][:], x[ob * 128:(ob + 1) * 128, t0:t0 + SLAB])]
                p1 = ps[ti % 6]
                mm = []
                for ib in range(HB):
                    mm.append(lambda s, ob=ob, c=c, ib=ib, p1=p1: s.emit(
                        "tensor", lambda e: e.matmul(
                            p1[:], whbuf[ob % 2][:, ib * 128:(ib + 1) * 128],
                            hbuf[ib][:, c * 512:(c + 1) * 512],
                            start=(ib == 0), stop=(ib == HB - 1))))
                ev = [lambda s, ob=ob, c=c, p1=p1: s.emit(
                    "vector", lambda e: e.scalar_tensor_tensor(
                        obuf[ob % 2][:, c * 512:(c + 1) * 512], p1[:], s3c[:, ob:ob + 1],
                        xresb[ob % 2][:, c * 512:(c + 1) * 512], ALU.mult, ALU.add))]
                st = []
                if c == SLAB // 512 - 1:
                    st = [lambda s, ob=ob, t0=t0: s.dma(
                        "sync", xn[ob * 128:(ob + 1) * 128, t0:t0 + SLAB], obuf[ob % 2][:])]
                slots.append([ld, mm, ev, st])
                ti += 1
        pipeN(s, slots)
    s.build(block, sem)
    es.close()
    return nc



def emit_flash(s, es, nc, qsb, ksb, vsb, o_out, seq, b0, prefix,
               kv_of_head, window=None, masks=None, nheads=HPC):
    QC = 512
    nqc = seq // QC
    ntk = seq // 128
    psc = [es.enter_context(nc.psum_tensor(f"{prefix}_ps{j}", [128, 512], F32))
           for j in range(3)]
    pso = [es.enter_context(nc.psum_tensor(f"{prefix}_po{j}", [128, 512], F32))
           for j in range(2)]
    pbuf = [es.enter_context(nc.sbuf_tensor(f"{prefix}_pb{j}", [128, QC], BF16))
            for j in range(3)]
    recip = es.enter_context(nc.sbuf_tensor(f"{prefix}_rc", [128, 8], F32))
    dcol = [es.enter_context(nc.sbuf_tensor(f"{prefix}_dc{j}", [128, 4], F32))
            for j in range(2)]
    rcol = [es.enter_context(nc.sbuf_tensor(f"{prefix}_rr{j}", [128, 4], F32))
            for j in range(2)]
    osb = [es.enter_context(nc.sbuf_tensor(f"{prefix}_ob{j}", [128, 4 * 64], BF16))
           for j in range(2)]
    ones1 = es.enter_context(nc.sbuf_tensor(f"{prefix}_o1", [1, 128], BF16))
    zrow = es.enter_context(nc.sbuf_tensor(f"{prefix}_zr", [1, 512], BF16))
    nb0c = const_col(s, es, nc, f"{prefix}_nb0", -b0)
    s.barrier()
    s.emit("vector", lambda e: e.memset(ones1[:], 1.0))
    s.emit("vector", lambda e: e.memset(zrow[:], 0.0))
    s.barrier()

    slots = []
    ci = 0
    ki = 0
    for h in range(nheads):
        qt, qoff = h // 2, (h % 2) * 64
        kt_i, koff, kvslot = kv_of_head[h]
        vbase = kvslot * ntk * 65
        for qc in range(nqc):
            q0 = qc * QC
            lo = 0 if window is None else max(0, (q0 - window + 1) // 128)
            hi = (q0 + QC) // 128
            slots.append([[], [lambda s, ci=ci: s.emit("tensor", lambda e: e.matmul(
                pso[ci % 2][:], ones1[:], zrow[:], start=True, stop=False))]])
            for kt in range(lo, hi):
                pi = ki % 3
                r = kt * 128 - q0
                mm_sc = [lambda s, qt=qt, qoff=qoff, kt_i=kt_i, koff=koff, kt=kt, qc=qc, pi=pi:
                         s.emit("tensor", lambda e: e.matmul(
                             psc[pi][:],
                             ksb[kt_i][koff:koff + 64, kt * 128:(kt + 1) * 128],
                             qsb[qt][qoff:qoff + 64, qc * QC:(qc + 1) * QC],
                             start=True, stop=True))]
                ev = [lambda s, pi=pi: s.emit("scalar", lambda e: e.activation(
                    pbuf[pi][:], psc[pi][:], AF.Exp, bias=nb0c[:], scale=0.125))]
                if window is None:
                    need_mask = r >= 0
                    mi = r // 128
                else:
                    need_mask = True
                    mi = (r + 256) // 128
                mk = []
                if need_mask:
                    mk = [lambda s, pi=pi, mi=mi: s.emit(
                        "vector", lambda e: e.tensor_mul(
                            pbuf[pi][:], pbuf[pi][:], masks[:, mi * 512:(mi + 1) * 512]))]
                last = (kt == hi - 1)
                pv = []
                for qs in range(4):
                    pv.append(lambda s, pi=pi, qs=qs, kt=kt, ci=ci, vbase=vbase, last=last:
                              s.emit("tensor", lambda e: e.matmul(
                                  pso[ci % 2][:, qs * 66:qs * 66 + 65],
                                  pbuf[pi][:, qs * 128:(qs + 1) * 128],
                                  vsb[:, vbase + kt * 65:vbase + (kt + 1) * 65],
                                  start=False, stop=(last and qs == 3))))
                slots.append([[], mm_sc, ev, mk, pv])
                ki += 1
            ev1 = [lambda s, qs=qs, ci=ci: s.emit("vector", lambda e: e.tensor_copy(
                dcol[ci % 2][:, qs:qs + 1],
                pso[ci % 2][:, qs * 66 + 64:qs * 66 + 65])) for qs in range(4)]
            ev15 = [lambda s, ci=ci: s.emit("vector", lambda e: e.reciprocal(
                rcol[ci % 2][:], dcol[ci % 2][:]))]
            ev2 = [lambda s, qs=qs, ci=ci: s.emit("vector", lambda e: e.tensor_scalar(
                osb[ci % 2][:, qs * 64:(qs + 1) * 64],
                pso[ci % 2][:, qs * 66:qs * 66 + 64],
                rcol[ci % 2][:, qs:qs + 1], None, ALU.mult))
                for qs in range(4)]
            st = [lambda s, h=h, qc=qc, ci=ci: s.dma(
                "sync",
                o_out[qc * QC:(qc + 1) * QC, h * 64:(h + 1) * 64]
                .rearrange("(qs p) d -> p qs d", p=128),
                osb[ci % 2][:].rearrange("p (qs d) -> p qs d", qs=4))]
            slots.append([[], [], [], [], ev1, ev15, ev2, st])
            ci += 1
    pipeN(s, slots)


def build_attn_a0():
    nc = bass.Bass("TRN2", target_bir_lowering=False)
    q = nc.dram_tensor("q", [256, S], BF16, kind="ExternalInput").ap()
    k = nc.dram_tensor("k", [256, S], BF16, kind="ExternalInput").ap()
    v = nc.dram_tensor("v", [128, 4 * (S // 128) * 65], BF16, kind="ExternalInput").ap()
    cm = nc.dram_tensor("cm", [128, 4 * 512], F32, kind="ExternalInput").ap()
    o = nc.dram_tensor("o", [S, 256], BF16, kind="ExternalOutput").ap()
    es = ExitStack()
    qsb = [es.enter_context(nc.sbuf_tensor(f"q{j}", [128, S], BF16)) for j in range(2)]
    ksb = [es.enter_context(nc.sbuf_tensor(f"k{j}", [128, S], BF16)) for j in range(2)]
    vsb = es.enter_context(nc.sbuf_tensor("vsb", [128, 4 * (S // 128) * 65], BF16))
    msk = es.enter_context(nc.sbuf_tensor("cmsk", [128, 4 * 512], BF16))
    sem, block = make_block(nc, es)
    s = St()
    s.barrier()
    for j in range(2):
        s.dma("sync", qsb[j][:], q[j * 128:(j + 1) * 128, :])
        s.dma("sync", ksb[j][:], k[j * 128:(j + 1) * 128, :])
    s.dma("gpsimd", vsb[:], v)
    s.dma("gpsimd", msk[:], cm)
    s.barrier()
    kv = {h: (h // 2, (h % 2) * 64, h) for h in range(4)}
    emit_flash(s, es, nc, qsb, ksb, vsb, o, S, B0_A0, "fa", kv, masks=msk)
    s.build(block, sem)
    es.close()
    return nc


def build_attn_b0(window):
    nm = 4 if window is None else 6
    nc = bass.Bass("TRN2", target_bir_lowering=False)
    q = nc.dram_tensor("q", [256, S], BF16, kind="ExternalInput").ap()
    k = nc.dram_tensor("k", [128, S], BF16, kind="ExternalInput").ap()
    v = nc.dram_tensor("v", [128, (S // 128) * 65], BF16, kind="ExternalInput").ap()
    mk = nc.dram_tensor("mk", [128, nm * 512], F32, kind="ExternalInput").ap()
    o = nc.dram_tensor("o", [S, 256], BF16, kind="ExternalOutput").ap()
    es = ExitStack()
    qsb = [es.enter_context(nc.sbuf_tensor(f"q{j}", [128, S], BF16)) for j in range(2)]
    ksb = [es.enter_context(nc.sbuf_tensor("k0", [128, S], BF16))]
    vsb = es.enter_context(nc.sbuf_tensor("vsb", [128, (S // 128) * 65], BF16))
    msk = es.enter_context(nc.sbuf_tensor("msk", [128, nm * 512], BF16))
    sem, block = make_block(nc, es)
    s = St()
    s.barrier()
    for j in range(2):
        s.dma("sync", qsb[j][:], q[j * 128:(j + 1) * 128, :])
    s.dma("sync", ksb[0][:], k)
    s.dma("gpsimd", vsb[:], v)
    s.dma("gpsimd", msk[:], mk)
    s.barrier()
    kv = {h: (0, (h % 2) * 64, 0) for h in range(4)}
    emit_flash(s, es, nc, qsb, ksb, vsb, o, S, B0_QKN, "fb", kv,
               window=window, masks=msk)
    s.build(block, sem)
    es.close()
    return nc



def emit_rope_qknorm(s, es, nc, strip, nheads_s, tokn, cos_sb, sin_sb, prefix,
                     eps=1e-6):
    ntt = tokn // 128
    Wd = nheads_s * 64
    Wh = nheads_s * 32
    les = ExitStack()
    t1 = les.enter_context(nc.sbuf_tensor(f"{prefix}_t1", [128, Wh], F32))
    t2 = les.enter_context(nc.sbuf_tensor(f"{prefix}_t2", [128, Wh], F32))
    t3 = les.enter_context(nc.sbuf_tensor(f"{prefix}_t3", [128, Wh], F32))
    t4 = les.enter_context(nc.sbuf_tensor(f"{prefix}_t4", [128, Wh], F32))
    sqb = les.enter_context(nc.sbuf_tensor(f"{prefix}_sqb", [128, Wd], F32))
    ss = les.enter_context(nc.sbuf_tensor(f"{prefix}_ss", [128, nheads_s], F32))
    epsc = const_col(s, les, nc, f"{prefix}_eps", eps)

    def ap_e(tt):
        return strip[:, tt * Wd:(tt + 1) * Wd].rearrange(
            "p (hf two) -> p hf two", two=2)[:, :, 0]

    def ap_o(tt):
        return strip[:, tt * Wd:(tt + 1) * Wd].rearrange(
            "p (hf two) -> p hf two", two=2)[:, :, 1]

    for tt in range(ntt):
        cosv = cos_sb[:, tt * Wh:(tt + 1) * Wh]
        sinv = sin_sb[:, tt * Wh:(tt + 1) * Wh]
        s.barrier()
        s.emit("vector", lambda e, tt=tt, cosv=cosv: e.tensor_mul(t1[:], ap_e(tt), cosv))
        s.emit("scalar", lambda e, tt=tt, sinv=sinv: e.copy(t3[:], sinv))
        s.barrier()
        s.emit("vector", lambda e, tt=tt, sinv=sinv: e.tensor_mul(t2[:], ap_o(tt), sinv))
        s.barrier()
        s.emit("vector", lambda e, tt=tt: e.tensor_mul(t3[:], ap_e(tt), t3[:]))
        s.barrier()
        s.emit("vector", lambda e, tt=tt, cosv=cosv: e.tensor_mul(t4[:], ap_o(tt), cosv))
        s.barrier()
        s.emit("vector", lambda e, tt=tt: e.tensor_tensor(ap_e(tt), t1[:], t2[:], ALU.subtract))
        s.barrier()
        s.emit("vector", lambda e, tt=tt: e.tensor_add(ap_o(tt), t3[:], t4[:]))
        s.barrier()
        s.emit("scalar", lambda e, tt=tt: e.activation(
            sqb[:], strip[:, tt * Wd:(tt + 1) * Wd], AF.Square))
        s.barrier()
        s.emit("vector", lambda e: e.reduce_sum(
            ss[:], sqb[:].rearrange("p (h d) -> p h d", d=64), axis=AX.X))
        s.barrier()
        s.emit("scalar", lambda e: e.activation(ss[:], ss[:], AF.Ln, bias=epsc[:], scale=1.0 / 64))
        s.barrier()
        s.emit("scalar", lambda e: e.activation(ss[:], ss[:], AF.Exp, scale=-0.5))
        s.barrier()
        for hh in range(nheads_s):
            s.emit("vector", lambda e, tt=tt, hh=hh: e.tensor_scalar(
                strip[:, tt * Wd + hh * 64:tt * Wd + (hh + 1) * 64],
                strip[:, tt * Wd + hh * 64:tt * Wd + (hh + 1) * 64],
                ss[:, hh:hh + 1], None, ALU.mult))
        s.barrier()
    les.close()


def build_b0_norms():
    nc = bass.Bass("TRN2", target_bir_lowering=False)
    x = nc.dram_tensor("x", [D, TOK], F32, kind="ExternalInput").ap()
    g1 = nc.dram_tensor("g1", [128, NB], F32, kind="ExternalInput").ap()
    n_out = nc.dram_tensor("n_out", [D, TOK], F32, kind="ExternalOutput").ap()
    m_out = nc.dram_tensor("m_out", [D, TOK], BF16, kind="ExternalOutput").ap()
    es = ExitStack()
    nhat = [es.enter_context(nc.sbuf_tensor(f"nh{i}", [128, TOK], BF16)) for i in range(NB)]
    mt = [es.enter_context(nc.sbuf_tensor(f"mt{i}", [128, TOK], BF16)) for i in range(2)]
    g1c = es.enter_context(nc.sbuf_tensor("g1c", [128, NB], F32))
    sem, block = make_block(nc, es)
    s = St()
    s.barrier()
    s.dma("sync", g1c[:], g1)
    emit_ln(s, es, nc, x, nhat, NB, TOK, out32_dram=n_out)
    emit_rms_sb(s, es, nc, nhat, mt, m_out, NB, TOK, g1c, "rms1")
    s.build(block, sem)
    es.close()
    return nc


def emit_rms_sb(s, es, nc, xsb, rot_out, out_dram, nblk, tokn, gamma_col,
                prefix, eps=1e-6):
    ncn = tokn // 512
    les = ExitStack()
    ones_bc = les.enter_context(nc.sbuf_tensor(f"{prefix}_ones", [128, 128], BF16))
    rb = les.enter_context(nc.sbuf_tensor(f"{prefix}_rb", [128, tokn], F32))
    e1 = les.enter_context(nc.sbuf_tensor(f"{prefix}_e1", [128, tokn], F32))
    sqs = [les.enter_context(nc.sbuf_tensor(f"{prefix}_sq{j}", [128, tokn], BF16))
           for j in range(2)]
    epsc = const_col(s, les, nc, f"{prefix}_eps", eps)
    with nc.psum_tensor(f"{prefix}_pm", [128, tokn], F32) as pm:
        s.barrier()
        s.emit("vector", lambda e: e.memset(ones_bc[:], 1.0))
        slots = []
        for i in range(nblk):
            sq_ops = [lambda s, i=i: s.emit("scalar", lambda e: e.activation(
                sqs[i % 2][:], xsb[i][:], AF.Square))]
            mm_ops = []
            for c in range(ncn):
                mm_ops.append(lambda s, i=i, c=c: s.emit("tensor", lambda e: e.matmul(
                    pm[:, c * 512:(c + 1) * 512], ones_bc[:],
                    sqs[i % 2][:, c * 512:(c + 1) * 512],
                    start=(i == 0), stop=(i == nblk - 1))))
            slots.append([sq_ops, mm_ops])
        pipeN(s, slots)
        nd = float(nblk * 128)
        s.barrier()
        s.emit("vector", lambda e: e.tensor_scalar(e1[:], pm[:], 1.0 / nd, None, ALU.mult))
        s.barrier()
        s.emit("scalar", lambda e: e.activation(e1[:], e1[:], AF.Ln, bias=epsc[:]))
        s.barrier()
        s.emit("scalar", lambda e: e.activation(rb[:], e1[:], AF.Exp, scale=-0.5))
        s.barrier()
        for i in range(nblk):
            s.emit("vector", lambda e, i=i: e.tensor_mul(e1[:], xsb[i][:], rb[:]))
            s.barrier()
            s.emit("vector", lambda e, i=i: e.tensor_scalar(
                rot_out[i % 2][:], e1[:], gamma_col[:, i:i + 1], None, ALU.mult))
            s.barrier()
            s.dma("sync", out_dram[i * 128:(i + 1) * 128, :], rot_out[i % 2][:])
            s.barrier()
    s.barrier()
    les.close()


QKV_DEBUG = 99

def build_b0_qkv(q_cols, latent):
    nc = bass.Bass("TRN2", target_bir_lowering=False)
    m = nc.dram_tensor("m", [D, TOK], BF16, kind="ExternalInput").ap()
    wq = nc.dram_tensor("wq", [D, q_cols], F32, kind="ExternalInput").ap()
    if latent:
        gkvd = nc.dram_tensor("gkvd", [D, 256], F32, kind="ExternalInput").ap()
        kvup = nc.dram_tensor("kvup", [256, 512], F32, kind="ExternalInput").ap()
    else:
        wkv = nc.dram_tensor("wkv", [D, 512], F32, kind="ExternalInput").ap()
    ntt = TOK // 128
    cosq = nc.dram_tensor("cosq", [128, ntt * H * 32], F32, kind="ExternalInput").ap()
    sinq = nc.dram_tensor("sinq", [128, ntt * H * 32], F32, kind="ExternalInput").ap()
    cosk = nc.dram_tensor("cosk", [128, ntt * 4 * 32], F32, kind="ExternalInput").ap()
    sink = nc.dram_tensor("sink", [128, ntt * 4 * 32], F32, kind="ExternalInput").ap()
    q_out = nc.dram_tensor("q_out", [TOK, q_cols], BF16, kind="ExternalOutput").ap()
    k_out = nc.dram_tensor("k_out", [TOK, 256], BF16, kind="ExternalOutput").ap()
    v_out = nc.dram_tensor("v_out", [TOK, 256], BF16, kind="ExternalOutput").ap()

    es = ExitStack()
    mt = [es.enter_context(nc.sbuf_tensor(f"mt{i}", [128, TOK], BF16)) for i in range(NB)]
    qstrip = es.enter_context(nc.sbuf_tensor("qstrip", [128, ntt * q_cols], BF16))
    kstrip = es.enter_context(nc.sbuf_tensor("kstrip", [128, ntt * 256], BF16))
    vstrip = es.enter_context(nc.sbuf_tensor("vstrip", [128, ntt * 256], BF16))
    cq = es.enter_context(nc.sbuf_tensor("cq", [128, ntt * H * 32], BF16))
    sq_ = es.enter_context(nc.sbuf_tensor("sq_", [128, ntt * H * 32], BF16))
    ck = es.enter_context(nc.sbuf_tensor("ck", [128, ntt * 4 * 32], BF16))
    sk = es.enter_context(nc.sbuf_tensor("sk", [128, ntt * 4 * 32], BF16))
    sem, block = make_block(nc, es)
    s = St()
    s.barrier()
    for i in range(NB):
        s.dma("sync", mt[i][:], m[i * 128:(i + 1) * 128, :])
    s.dma("gpsimd", cq[:], cosq)
    s.dma("gpsimd", sq_[:], sinq)
    s.dma("gpsimd", ck[:], cosk)
    s.dma("gpsimd", sk[:], sink)
    s.barrier()

    wq_sb = load_w_tm(s, es, nc, wq, NB, q_cols, "wq")

    def ep_q(tt, ps):
        return [[lambda s, tt=tt, ps=ps: s.emit("vector", lambda e: e.tensor_copy(
            qstrip[:, tt * q_cols:(tt + 1) * q_cols], ps[:, :q_cols]))]]
    emit_proj_tm(s, es, nc, mt, wq_sb, NB, q_cols, TOK, "qp", ep_q)

    if QKV_DEBUG < 2:
        pass
    elif latent:
        lat = [es.enter_context(nc.sbuf_tensor(f"lat{i}", [128, TOK], BF16))
               for i in range(2)]
        emit_proj_cm(s, es, nc, mt, gkvd, None, NB, 2, TOK, "lat", sb_out=lat)
        if QKV_DEBUG != 2:
            kvu_sb = load_w_tm(s, es, nc, kvup, 2, 512, "kvu")

            def ep_kv(tt, ps):
                return [[lambda s, tt=tt, ps=ps: s.emit("vector", lambda e: e.tensor_copy(
                    kstrip[:, tt * 256:(tt + 1) * 256], ps[:, 0:256])),
                    lambda s, tt=tt, ps=ps: s.emit("vector", lambda e: e.tensor_copy(
                    vstrip[:, tt * 256:(tt + 1) * 256], ps[:, 256:512]))]]
            emit_proj_tm(s, es, nc, lat, kvu_sb, 2, 512, TOK, "kvp", ep_kv)
    else:
        wkv_sb = load_w_tm(s, es, nc, wkv, NB, 512, "wkv")

        def ep_kv(tt, ps):
            return [[lambda s, tt=tt, ps=ps: s.emit("vector", lambda e: e.tensor_copy(
                kstrip[:, tt * 256:(tt + 1) * 256], ps[:, 0:256])),
                lambda s, tt=tt, ps=ps: s.emit("vector", lambda e: e.tensor_copy(
                vstrip[:, tt * 256:(tt + 1) * 256], ps[:, 256:512]))]]
        emit_proj_tm(s, es, nc, mt, wkv_sb, NB, 512, TOK, "kvp", ep_kv)

    if QKV_DEBUG >= 3:
        emit_rope_qknorm(s, es, nc, qstrip, H, TOK, cq, sq_, "rq")
    if QKV_DEBUG >= 4:
        emit_rope_qknorm(s, es, nc, kstrip, 4, TOK, ck, sk, "rk")
    s.barrier()
    for tt in range(ntt):
        s.dma("sync", q_out[tt * 128:(tt + 1) * 128, :],
              qstrip[:, tt * q_cols:(tt + 1) * q_cols])
        s.dma("sync", k_out[tt * 128:(tt + 1) * 128, :],
              kstrip[:, tt * 256:(tt + 1) * 256])
        s.dma("sync", v_out[tt * 128:(tt + 1) * 128, :],
              vstrip[:, tt * 256:(tt + 1) * 256])
    s.build(block, sem)
    es.close()
    return nc


def build_b0_mid():
    nc = bass.Bass("TRN2", target_bir_lowering=False)
    og = nc.dram_tensor("og", [D, TOK], BF16, kind="ExternalInput").ap()
    gwo = nc.dram_tensor("gwo", [D, D], F32, kind="ExternalInput").ap()
    n_in = nc.dram_tensor("n_in", [D, TOK], F32, kind="ExternalInput").ap()
    x = nc.dram_tensor("x", [D, TOK], F32, kind="ExternalInput").ap()
    g2 = nc.dram_tensor("g2", [128, NB], F32, kind="ExternalInput").ap()
    xp_out = nc.dram_tensor("xp_out", [D, TOK], F32, kind="ExternalOutput").ap()
    m_out = nc.dram_tensor("m_out", [D, TOK], BF16, kind="ExternalOutput").ap()
    es = ExitStack()
    ohat = [es.enter_context(nc.sbuf_tensor(f"oh{i}", [128, TOK], BF16)) for i in range(NB)]
    h2sb = [es.enter_context(nc.sbuf_tensor(f"h2_{i}", [128, TOK], BF16)) for i in range(NB)]
    mt = [es.enter_context(nc.sbuf_tensor(f"mt{i}", [128, TOK], BF16)) for i in range(2)]
    g2c = es.enter_context(nc.sbuf_tensor("g2c", [128, NB], F32))
    sem, block = make_block(nc, es)
    s = St()
    s.barrier()
    s.dma("sync", g2c[:], g2)
    for i in range(NB):
        s.dma("gpsimd", ohat[i][:], og[i * 128:(i + 1) * 128, :])
    s.barrier()

    def epi(ot, ob):
        return [lambda s, ot=ot, ob=ob: s.emit("scalar", lambda e: e.copy(
            h2sb[ot][:], ob[:]))]
    emit_proj_cm(s, es, nc, ohat, gwo, None, NB, NB, TOK, "go",
                 x_dram=n_in, out2_dram=xp_out, x2_dram=x, epilogue=epi)
    emit_rms_sb(s, es, nc, h2sb, mt, m_out, NB, TOK, g2c, "rms2")
    s.build(block, sem)
    es.close()
    return nc



def build_a1_pre():
    nc = bass.Bass("TRN2", target_bir_lowering=False)
    x = nc.dram_tensor("x", [D, TOK], F32, kind="ExternalInput").ap()
    wq = nc.dram_tensor("wq", [D, D], F32, kind="ExternalInput").ap()
    wk = nc.dram_tensor("wk", [D, D], F32, kind="ExternalInput").ap()
    wv = nc.dram_tensor("wv", [D, D], F32, kind="ExternalInput").ap()
    q_out = nc.dram_tensor("q_out", [TOK, D], BF16, kind="ExternalOutput").ap()
    k_out = nc.dram_tensor("k_out", [TOK, D], BF16, kind="ExternalOutput").ap()
    v_out = nc.dram_tensor("v_out", [TOK, D], BF16, kind="ExternalOutput").ap()
    ntt = TOK // 128
    es = ExitStack()
    xhat = [es.enter_context(nc.sbuf_tensor(f"xhat{i}", [128, TOK], BF16)) for i in range(NB)]
    qstrip = es.enter_context(nc.sbuf_tensor("qstrip", [128, ntt * D], BF16))
    kstrip = es.enter_context(nc.sbuf_tensor("kstrip", [128, ntt * D], BF16))
    vstrip = es.enter_context(nc.sbuf_tensor("vstrip", [128, ntt * D], BF16))
    sem, block = make_block(nc, es)
    s = St()
    emit_ln(s, es, nc, x, xhat, NB, TOK)
    for (w, strip, nm) in ((wq, qstrip, "qp"), (wk, kstrip, "kp"), (wv, vstrip, "vp")):
        wsb = load_w_tm(s, es, nc, w, NB, D, nm)

        def ep(tt, ps, strip=strip):
            return [[lambda s, tt=tt, ps=ps: s.emit("vector", lambda e: e.tensor_copy(
                strip[:, tt * D:(tt + 1) * D], ps[:, :D]))]]
        emit_proj_tm(s, es, nc, xhat, wsb, NB, D, TOK, nm, ep)
    emit_knorm(s, es, nc, kstrip, H, TOK, "kn")
    s.barrier()
    for tt in range(ntt):
        s.dma("sync", q_out[tt * 128:(tt + 1) * 128, :], qstrip[:, tt * D:(tt + 1) * D])
        s.dma("sync", k_out[tt * 128:(tt + 1) * 128, :], kstrip[:, tt * D:(tt + 1) * D])
        s.dma("sync", v_out[tt * 128:(tt + 1) * 128, :], vstrip[:, tt * D:(tt + 1) * D])
    s.build(block, sem)
    es.close()
    return nc


def emit_knorm(s, es, nc, strip, nheads_s, tokn, prefix):
    ntt = tokn // 128
    Wd = nheads_s * 64
    les = ExitStack()
    sqb = les.enter_context(nc.sbuf_tensor(f"{prefix}_sqb", [128, Wd], F32))
    ss = les.enter_context(nc.sbuf_tensor(f"{prefix}_ss", [128, nheads_s], F32))
    epsc = const_col(s, les, nc, f"{prefix}_eps", 1e-30)
    for tt in range(ntt):
        s.barrier()
        s.emit("scalar", lambda e, tt=tt: e.activation(
            sqb[:], strip[:, tt * Wd:(tt + 1) * Wd], AF.Square))
        s.barrier()
        s.emit("vector", lambda e: e.reduce_sum(
            ss[:], sqb[:].rearrange("p (h d) -> p h d", d=64), axis=AX.X))
        s.barrier()
        s.emit("scalar", lambda e: e.activation(ss[:], ss[:], AF.Ln, bias=epsc[:]))
        s.barrier()
        s.emit("scalar", lambda e: e.activation(ss[:], ss[:], AF.Exp, scale=-0.5))
        s.barrier()
        s.emit("vector", lambda e: e.tensor_scalar_min(ss[:], ss[:], 1e12))
        s.barrier()
        for hh in range(nheads_s):
            s.emit("vector", lambda e, tt=tt, hh=hh: e.tensor_scalar(
                strip[:, tt * Wd + hh * 64:tt * Wd + (hh + 1) * 64],
                strip[:, tt * Wd + hh * 64:tt * Wd + (hh + 1) * 64],
                ss[:, hh:hh + 1], None, ALU.mult))
        s.barrier()
    les.close()


def build_a1_scan():
    nc = bass.Bass("TRN2", target_bir_lowering=False)
    q = nc.dram_tensor("q", [256, S], BF16, kind="ExternalInput").ap()
    k = nc.dram_tensor("k", [256, S], BF16, kind="ExternalInput").ap()
    ktm = nc.dram_tensor("ktm", [128, 4 * (S // 128) * 64], BF16, kind="ExternalInput").ap()
    vtm = nc.dram_tensor("vtm", [128, 4 * (S // 128) * 64], BF16, kind="ExternalInput").ap()
    araw = nc.dram_tensor("araw", [128, HPC], F32, kind="ExternalInput").ap()
    braw = nc.dram_tensor("braw", [128, HPC], F32, kind="ExternalInput").ap()
    mlow = nc.dram_tensor("mlow", [128, 128], F32, kind="ExternalInput").ap()
    mup = nc.dram_tensor("mup", [128, 128], F32, kind="ExternalInput").ap()
    iden = nc.dram_tensor("iden", [128, 128], F32, kind="ExternalInput").ap()
    y = nc.dram_tensor("y", [S, 256], BF16, kind="ExternalOutput").ap()

    NCH = S // CS
    es = ExitStack()
    qhb = [es.enter_context(nc.sbuf_tensor(f"qh{h}", [64, S], BF16)) for h in range(4)]
    khb = [es.enter_context(nc.sbuf_tensor(f"kh{h}", [64, S], BF16)) for h in range(4)]
    ktms = es.enter_context(nc.sbuf_tensor("ktms", [128, 4 * NCH * 64], BF16))
    vtms = es.enter_context(nc.sbuf_tensor("vtms", [128, 4 * NCH * 64], BF16))
    acol = es.enter_context(nc.sbuf_tensor("acol", [128, HPC], F32))
    nacol = es.enter_context(nc.sbuf_tensor("nacol", [128, HPC], F32))
    bcol = es.enter_context(nc.sbuf_tensor("bcol", [128, HPC], F32))
    sgt = es.enter_context(nc.sbuf_tensor("sgt", [128, HPC], F32))
    mL = es.enter_context(nc.sbuf_tensor("mL", [128, 128], BF16))
    mU = es.enter_context(nc.sbuf_tensor("mU", [128, 128], BF16))
    idn = es.enter_context(nc.sbuf_tensor("idn", [128, 128], BF16))
    ones1 = es.enter_context(nc.sbuf_tensor("ones1", [1, 128], BF16))
    zrow = es.enter_context(nc.sbuf_tensor("zrow", [1, 512], BF16))
    Nsb = [es.enter_context(nc.sbuf_tensor(f"Nsb{j}", [128, 128], BF16)) for j in range(2)]
    Ntsb = [es.enter_context(nc.sbuf_tensor(f"Ntsb{j}", [128, 128], BF16)) for j in range(2)]
    Psb = [[es.enter_context(nc.sbuf_tensor(f"P{j}_{l}", [128, 128], BF16))
            for l in range(6)] for j in range(2)]
    Ptsb = [[es.enter_context(nc.sbuf_tensor(f"Pt{j}_{l}", [128, 128], BF16))
             for l in range(7)] for j in range(2)]
    Mrm = [es.enter_context(nc.sbuf_tensor(f"Mrm{j}", [64, 64], BF16)) for j in range(2)]
    Mcm = [es.enter_context(nc.sbuf_tensor(f"Mcm{j}", [64, 64], BF16)) for j in range(2)]
    Mna = [es.enter_context(nc.sbuf_tensor(f"Mna{j}", [64, 64], BF16)) for j in range(2)]
    Vb = [es.enter_context(nc.sbuf_tensor(f"Vb{j}", [128, 64], BF16)) for j in range(2)]
    ub = [[es.enter_context(nc.sbuf_tensor(f"u{j}_{t}", [128, 64], BF16))
           for t in range(2)] for j in range(2)]
    Dsb = [es.enter_context(nc.sbuf_tensor(f"Dsb{j}", [128, 64], BF16)) for j in range(2)]
    Gsb = [es.enter_context(nc.sbuf_tensor(f"Gsb{j}", [128, 128], BF16)) for j in range(2)]
    ysb = [es.enter_context(nc.sbuf_tensor(f"ysb{j}", [128, 64], BF16)) for j in range(2)]
    psS = es.enter_context(nc.psum_tensor("psS", [128, 512], F32))
    psP = [es.enter_context(nc.psum_tensor(f"psP{t}", [128, 512], F32)) for t in range(2)]
    ptM = es.enter_context(nc.psum_tensor("ptM", [128, 1024], BF16))
    Mps = es.enter_context(nc.psum_tensor("Mps", [128, 512], F32))
    psR = es.enter_context(nc.psum_tensor("psR", [128, 512], F32))
    psY = es.enter_context(nc.psum_tensor("psY", [128, 512], F32))
    psU = es.enter_context(nc.psum_tensor("psU", [128, 512], F32))

    sem, block = make_block(nc, es)
    s = St()
    s.barrier()
    for h in range(4):
        s.dma("sync", qhb[h][:], q[h * 64:(h + 1) * 64, :])
        s.dma("sync", khb[h][:], k[h * 64:(h + 1) * 64, :])
    s.dma("gpsimd", ktms[:], ktm)
    s.dma("gpsimd", vtms[:], vtm)
    s.dma("sync", acol[:], araw)
    s.dma("sync", bcol[:], braw)
    s.dma("gpsimd", mL[:], mlow)
    s.dma("gpsimd", mU[:], mup)
    s.dma("gpsimd", idn[:], iden)
    s.barrier()
    s.emit("scalar", lambda e: e.activation(sgt[:], acol[:], AF.Sigmoid))
    s.emit("vector", lambda e: e.memset(ones1[:], 1.0))
    s.emit("vector", lambda e: e.memset(zrow[:], 0.0))
    s.barrier()
    s.emit("vector", lambda e: e.tensor_scalar(acol[:], sgt[:], 0.5, None, ALU.mult))
    s.barrier()
    s.emit("scalar", lambda e: e.activation(sgt[:], bcol[:], AF.Sigmoid))
    s.barrier()
    s.emit("vector", lambda e: e.tensor_scalar(nacol[:], acol[:], -1.0, None, ALU.mult))
    s.emit("vector", lambda e: e.tensor_scalar(bcol[:], sgt[:], 0.5, None, ALU.mult))
    s.barrier()

    id64 = idn[0:64, 0:64]
    for g in range(2):
        s.barrier()
        s.emit("tensor", lambda e: e.matmul(
            Mps[0:64, 0:128], ones1[:, 0:64], zrow[:, 0:128], start=True, stop=False))
        s.barrier()
        for c in range(NCH):
            t0 = c * CS
            hh = [2 * g, 2 * g + 1]
            kap = [khb[hh[j]][:, t0:t0 + CS] for j in range(2)]
            qap = [qhb[hh[j]][:, t0:t0 + CS] for j in range(2)]
            vap = [vtms[:, hh[j] * NCH * 64 + c * 64:hh[j] * NCH * 64 + (c + 1) * 64]
                   for j in range(2)]
            ktap = [ktms[:, hh[j] * NCH * 64 + c * 64:hh[j] * NCH * 64 + (c + 1) * 64]
                    for j in range(2)]

            s.barrier()
            for j in range(2):
                s.emit("tensor", lambda e, j=j, ka=kap[j]: e.matmul(
                    psS[:, j * 128:(j + 1) * 128], ka, ka,
                    start=True, stop=True))
                s.emit("vector", lambda e, j=j, va=vap[j], hj=hh[j]: e.tensor_scalar(
                    Vb[j][:], va, bcol[:, hj:hj + 1], None, ALU.mult))
                if c == 0:
                    s.emit("gpsimd", lambda e, j=j: e.memset(Mrm[j][:], 0.0))
                else:
                    s.emit("scalar", lambda e, j=j: e.copy(
                        Mrm[j][:], Mps[0:64, j * 64:(j + 1) * 64]))
            s.barrier()
            for j in range(2):
                s.emit("vector", lambda e, j=j, hj=hh[j]: e.scalar_tensor_tensor(
                    Nsb[j][:], psS[:, j * 128:(j + 1) * 128],
                    acol[:, hj:hj + 1], mL[:], ALU.mult, ALU.mult))
                s.emit("vector", lambda e, j=j, hj=hh[j]: e.tensor_scalar(
                    Ntsb[j][:], psS[:, j * 128:(j + 1) * 128],
                    acol[:, hj:hj + 1], None, ALU.mult))
                s.emit("tensor", lambda e, j=j: e.transpose(
                    ptM[0:64, j * 64:(j + 1) * 64], Mrm[j][:], id64))
            s.barrier()
            for j in range(2):
                s.emit("vector", lambda e, j=j: e.tensor_mul(Ntsb[j][:], Ntsb[j][:], mU[:]))
                s.emit("vector", lambda e, j=j: e.tensor_copy(
                    Mcm[j][:], ptM[0:64, j * 64:(j + 1) * 64]))
            s.barrier()
            for j in range(2):
                s.emit("vector", lambda e, j=j, hj=hh[j]: e.tensor_scalar(
                    Mna[j][:], Mcm[j][:], nacol[0:64, hj:hj + 1], None, ALU.mult))
                s.emit("tensor", lambda e, j=j: e.matmul(
                    psP[0][:, j * 128:(j + 1) * 128], Ntsb[j][:], Nsb[j][:],
                    start=True, stop=True))
                s.emit("tensor", lambda e, j=j: e.matmul(
                    psP[0][:, 256 + j * 128:256 + (j + 1) * 128], Nsb[j][:], Ntsb[j][:],
                    start=True, stop=True))
            s.barrier()
            for j in range(2):
                s.emit("vector", lambda e, j=j: e.tensor_copy(
                    Psb[j][0][:], psP[0][:, j * 128:(j + 1) * 128]))
                s.emit("vector", lambda e, j=j: e.tensor_copy(
                    Ptsb[j][0][:], psP[0][:, 256 + j * 128:256 + (j + 1) * 128]))
                s.emit("tensor", lambda e, j=j: e.matmul(
                    psR[:, j * 64:(j + 1) * 64], idn[:], Vb[j][:],
                    start=True, stop=False))
                s.emit("tensor", lambda e, j=j, ka=kap[j]: e.matmul(
                    psR[:, j * 64:(j + 1) * 64], ka, Mna[j][:],
                    start=False, stop=True))
            s.barrier()
            for j in range(2):
                s.emit("vector", lambda e, j=j: e.tensor_copy(
                    ub[j][0][:], psR[:, j * 64:(j + 1) * 64]))
            for lvl in range(1, 6):
                for j in range(2):
                    s.emit("tensor", lambda e, j=j, lvl=lvl: e.matmul(
                        psP[lvl % 2][:, j * 128:(j + 1) * 128],
                        Ptsb[j][lvl - 1][:], Psb[j][lvl - 1][:],
                        start=True, stop=True))
                    s.emit("tensor", lambda e, j=j, lvl=lvl: e.matmul(
                        psP[lvl % 2][:, 256 + j * 128:256 + (j + 1) * 128],
                        Psb[j][lvl - 1][:], Ptsb[j][lvl - 1][:],
                        start=True, stop=True))
                s.barrier()
                for j in range(2):
                    if lvl < 5:
                        s.emit("vector", lambda e, j=j, lvl=lvl: e.tensor_copy(
                            Psb[j][lvl][:], psP[lvl % 2][:, j * 128:(j + 1) * 128]))
                    s.emit("vector", lambda e, j=j, lvl=lvl: e.tensor_copy(
                        Ptsb[j][lvl][:], psP[lvl % 2][:, 256 + j * 128:256 + (j + 1) * 128]))
                s.barrier()
            cur = [0, 0]
            for ai, lvl in enumerate(range(5, -1, -1)):
                for j in range(2):
                    s.emit("tensor", lambda e, j=j, lvl=lvl, cu=cur[j]: e.matmul(
                        psU[:, j * 64:(j + 1) * 64], Ptsb[j][lvl][:], ub[j][cu][:],
                        start=True, stop=True))
                s.barrier()
                for j in range(2):
                    s.emit("vector", lambda e, j=j, cu=cur[j]: e.tensor_tensor(
                        ub[j][1 - cu][:], ub[j][cu][:], psU[:, j * 64:(j + 1) * 64],
                        ALU.add))
                    cur[j] = 1 - cur[j]
                s.barrier()
            for j in range(2):
                s.emit("tensor", lambda e, j=j, cu=cur[j]: e.matmul(
                    psU[:, j * 64:(j + 1) * 64], Ntsb[j][:], ub[j][cu][:],
                    start=True, stop=True))
            s.barrier()
            for j in range(2):
                s.emit("vector", lambda e, j=j, cu=cur[j]: e.tensor_tensor(
                    Dsb[j][:], ub[j][cu][:], psU[:, j * 64:(j + 1) * 64],
                    ALU.subtract))
                s.emit("tensor", lambda e, j=j, ka=kap[j], qa=qap[j]: e.matmul(
                    psS[:, j * 128:(j + 1) * 128], ka, qa,
                    start=True, stop=True))
            s.barrier()
            for j in range(2):
                s.emit("vector", lambda e, j=j: e.tensor_mul(
                    Gsb[j][:], psS[:, j * 128:(j + 1) * 128], mU[:]))
            s.barrier()
            for j in range(2):
                s.emit("tensor", lambda e, j=j: e.matmul(
                    psY[:, j * 64:(j + 1) * 64], Gsb[j][:], Dsb[j][:],
                    start=True, stop=False))
                s.emit("tensor", lambda e, j=j, qa=qap[j]: e.matmul(
                    psY[:, j * 64:(j + 1) * 64], qa, Mcm[j][:],
                    start=False, stop=True))
            s.barrier()
            for j in range(2):
                s.emit("vector", lambda e, j=j: e.tensor_copy(
                    ysb[j][:], psY[:, j * 64:(j + 1) * 64]))
                s.emit("tensor", lambda e, j=j, kt_=ktap[j], c=c: e.matmul(
                    Mps[0:64, j * 64:(j + 1) * 64], Dsb[j][:], kt_,
                    start=False, stop=(c == NCH - 1)))
            s.barrier()
            for j in range(2):
                s.dma("sync", y[t0:t0 + CS, hh[j] * 64:(hh[j] + 1) * 64], ysb[j][:])
            s.barrier()
    s.build(block, sem)
    es.close()
    return nc



def build_b1_pre():
    nc = bass.Bass("TRN2", target_bir_lowering=False)
    x = nc.dram_tensor("x", [D, TOK], F32, kind="ExternalInput").ap()
    wq = nc.dram_tensor("wq", [D, D], F32, kind="ExternalInput").ap()
    wk = nc.dram_tensor("wk", [D, D], F32, kind="ExternalInput").ap()
    wv = nc.dram_tensor("wv", [D, D], F32, kind="ExternalInput").ap()
    wg = nc.dram_tensor("wg", [D, D], F32, kind="ExternalInput").ap()
    bg = nc.dram_tensor("bg", [128, NB], F32, kind="ExternalInput").ap()
    q_out = nc.dram_tensor("q_out", [D, TOK], BF16, kind="ExternalOutput").ap()
    k_out = nc.dram_tensor("k_out", [D, TOK], BF16, kind="ExternalOutput").ap()
    v_out = nc.dram_tensor("v_out", [TOK, D], BF16, kind="ExternalOutput").ap()
    lg_out = nc.dram_tensor("lg_out", [D, TOK], F32, kind="ExternalOutput").ap()
    ntt = TOK // 128
    es = ExitStack()
    xhat = [es.enter_context(nc.sbuf_tensor(f"xhat{i}", [128, TOK], BF16)) for i in range(NB)]
    gsb = [es.enter_context(nc.sbuf_tensor(f"gsb{i}", [128, TOK], F32)) for i in range(NB)]
    bgc = es.enter_context(nc.sbuf_tensor("bgc", [128, NB], F32))
    sA = es.enter_context(nc.sbuf_tensor("sA", [128, TOK], F32))
    sM = es.enter_context(nc.sbuf_tensor("sM", [128, TOK], F32))
    sem, block = make_block(nc, es)
    s = St()
    onec = const_col(s, es, nc, "onec", 1.0)
    s.barrier()
    s.dma("sync", bgc[:], bg)
    emit_ln(s, es, nc, x, xhat, NB, TOK)
    emit_proj_cm(s, es, nc, xhat, wq, q_out, NB, NB, TOK, "qp")
    emit_proj_cm(s, es, nc, xhat, wk, k_out, NB, NB, TOK, "kp")
    vstrip = es.enter_context(nc.sbuf_tensor("vstrip", [128, ntt * D], BF16))
    wv_sb = load_w_tm(s, es, nc, wv, NB, D, "wv")

    def ep_v(tt, ps):
        return [[lambda s, tt=tt, ps=ps: s.emit("vector", lambda e: e.tensor_copy(
            vstrip[:, tt * D:(tt + 1) * D], ps[:, :D]))]]
    emit_proj_tm(s, es, nc, xhat, wv_sb, NB, D, TOK, "vp", ep_v)
    s.barrier()
    for tt in range(ntt):
        s.dma("sync", v_out[tt * 128:(tt + 1) * 128, :], vstrip[:, tt * D:(tt + 1) * D])

    def ev_g(ot, ps, ob):
        return [lambda s, ot=ot, ps=ps: s.emit("vector", lambda e: e.tensor_scalar(
            gsb[ot][:], ps[:], bgc[:, ot:ot + 1], None, ALU.add))]
    emit_proj_cm(s, es, nc, xhat, wg, None, NB, NB, TOK, "gp", ev_override=ev_g)
    for i in range(NB):
        s.barrier()
        s.emit("scalar", lambda e, i=i: e.activation(sA[:], gsb[i][:], AF.Abs))
        s.barrier()
        s.emit("vector", lambda e, i=i: e.tensor_scalar_min(sM[:], gsb[i][:], 0.0))
        s.emit("scalar", lambda e: e.activation(sA[:], sA[:], AF.Exp, scale=-1.0))
        s.barrier()
        s.emit("scalar", lambda e: e.activation(sA[:], sA[:], AF.Ln, bias=onec[:]))
        s.barrier()
        s.emit("vector", lambda e, i=i: e.tensor_tensor(gsb[i][:], sM[:], sA[:], ALU.subtract))
        s.barrier()
        s.dma("sync", lg_out[i * 128:(i + 1) * 128, :], gsb[i][:])
    s.build(block, sem)
    es.close()
    return nc


B1_DEBUG = 99
B1_NCH = None

def build_b1_scan():
    nc = bass.Bass("TRN2", target_bir_lowering=False)
    q = nc.dram_tensor("q", [256, S], BF16, kind="ExternalInput").ap()
    k = nc.dram_tensor("k", [256, S], BF16, kind="ExternalInput").ap()
    vtm = nc.dram_tensor("vtm", [128, 4 * (S // 128) * 64], BF16, kind="ExternalInput").ap()
    lg = nc.dram_tensor("lg", [256, S], F32, kind="ExternalInput").ap()
    muppi = nc.dram_tensor("muppi", [128, 128], F32, kind="ExternalInput").ap()
    iden = nc.dram_tensor("iden", [128, 128], F32, kind="ExternalInput").ap()
    o = nc.dram_tensor("o", [S, 256], BF16, kind="ExternalOutput").ap()

    NCH = S // CS
    NRUN = B1_NCH if B1_NCH is not None else NCH
    es = ExitStack()
    qsb = [es.enter_context(nc.sbuf_tensor(f"q{j}", [128, S], BF16)) for j in range(2)]
    ksb = [es.enter_context(nc.sbuf_tensor(f"k{j}", [128, S], BF16)) for j in range(2)]
    vtms = es.enter_context(nc.sbuf_tensor("vtms", [128, 4 * NCH * 64], BF16))
    lgsb = [es.enter_context(nc.sbuf_tensor(f"lg{j}", [128, S], F32)) for j in range(2)]
    mUi = es.enter_context(nc.sbuf_tensor("mUi", [128, 128], BF16))
    idn = es.enter_context(nc.sbuf_tensor("idn", [128, 128], BF16))
    zer = es.enter_context(nc.sbuf_tensor("zer", [128, 128], F32))
    Lh = [es.enter_context(nc.sbuf_tensor(f"Lh{j2}", [128, 128], F32)) for j2 in range(2)]
    Lc = [es.enter_context(nc.sbuf_tensor(f"Lc{j2}", [128, 128], F32)) for j2 in range(2)]
    Pq = [es.enter_context(nc.sbuf_tensor(f"Pq{j2}", [128, 128], F32)) for j2 in range(2)]
    Pk = [es.enter_context(nc.sbuf_tensor(f"Pk{j2}", [128, 128], F32)) for j2 in range(2)]
    qt = [es.enter_context(nc.sbuf_tensor(f"qt{j2}", [128, 128], BF16)) for j2 in range(2)]
    kt = [es.enter_context(nc.sbuf_tensor(f"kt{j2}", [128, 128], BF16)) for j2 in range(2)]
    ktm = [es.enter_context(nc.sbuf_tensor(f"ktm{h}", [128, 64], BF16)) for h in range(4)]
    Pm = [es.enter_context(nc.sbuf_tensor(f"Pm{h}", [128, 128], BF16)) for h in range(4)]
    S0 = [es.enter_context(nc.sbuf_tensor(f"S0_{h}", [128, 64], BF16)) for h in range(4)]
    osb = [es.enter_context(nc.sbuf_tensor(f"osb{h}", [128, 64], BF16)) for h in range(4)]
    psS = es.enter_context(nc.psum_tensor("psS", [128, 512], F32))
    ptK = es.enter_context(nc.psum_tensor("ptK", [128, 1024], BF16))
    psO = es.enter_context(nc.psum_tensor("psO", [128, 512], F32))
    psT = es.enter_context(nc.psum_tensor("psT", [128, 512], F32))

    sem, block = make_block(nc, es)
    s = St()
    s.barrier()
    for j in range(2):
        s.dma("sync", qsb[j][:], q[j * 128:(j + 1) * 128, :])
        s.dma("sync", ksb[j][:], k[j * 128:(j + 1) * 128, :])
        s.dma("sync", lgsb[j][:], lg[j * 128:(j + 1) * 128, :])
    s.dma("gpsimd", vtms[:], vtm)
    s.dma("gpsimd", mUi[:], muppi)
    s.dma("gpsimd", idn[:], iden)
    s.barrier()
    s.emit("vector", lambda e: e.memset(zer[:], 0.0))
    for h in range(4):
        po = (h % 2) * 64
        s.emit("gpsimd", lambda e, h=h, po=po: e.memset(S0[h][po:po + 64, :], 0.0))
    s.barrier()

    def hs(h):
        return (h // 2, (h % 2) * 64)

    for c in range(NRUN):
        t0 = c * CS
        vap = [vtms[:, h * NCH * 64 + c * 64:h * NCH * 64 + (c + 1) * 64]
               for h in range(4)]
        lgap = [lgsb[hs(h)[0]][hs(h)[1]:hs(h)[1] + 64, t0:t0 + CS] for h in range(4)]
        zap = [zer[hs(h)[1]:hs(h)[1] + 64, :] for h in range(4)]
        qcap = [qsb[j2][:, t0:t0 + CS] for j2 in range(2)]
        kcap = [ksb[j2][:, t0:t0 + CS] for j2 in range(2)]

        s.barrier()
        for h in range(4):
            ti, po = hs(h)
            s.emit("vector", lambda e, h=h, po=po, lga=lgap[h], za=zap[h]:
                   e.tensor_tensor_scan(
                Lh[h // 2][po:po + 64, :], lga, za, 0.0, ALU.add, ALU.add))
        s.barrier()
        for j2 in range(2):
            s.emit("scalar", lambda e, j2=j2: e.activation(Pq[j2][:], Lh[j2][:], AF.Exp))
            s.emit("vector", lambda e, j2=j2: e.tensor_scalar_max(Lc[j2][:], Lh[j2][:], -60.0))
        s.barrier()
        for j2 in range(2):
            s.emit("scalar", lambda e, j2=j2: e.activation(Pk[j2][:], Lc[j2][:], AF.Exp, scale=-1.0))
            s.emit("vector", lambda e, j2=j2, qa=qcap[j2]: e.tensor_mul(
                qt[j2][:], qa, Pq[j2][:]))
        s.barrier()
        for j2 in range(2):
            s.emit("vector", lambda e, j2=j2, ka=kcap[j2]: e.tensor_mul(
                kt[j2][:], ka, Pk[j2][:]))
        s.barrier()
        for h in (0, 2, 1, 3):
            ti, po = hs(h)
            if h == 1:
                s.emit("tensor", lambda e: e.matmul(
                    psT[0:128, 448:512], idn[:], idn[:, 0:64],
                    start=True, stop=True))
            s.emit("tensor", lambda e, h=h, ti=ti, po=po: e.matmul(
                psS[:, h * 128:(h + 1) * 128], kt[ti][po:po + 64, :],
                qt[ti][po:po + 64, :], start=True, stop=True))
            s.emit("tensor", lambda e, h=h, ti=ti, po=po: e.transpose(
                ptK[:, h * 64:(h + 1) * 64], kt[ti][po:po + 64, :],
                idn[po:po + 64, po:po + 64]))
        s.barrier()
        for h in range(4):
            if B1_DEBUG >= 2 or B1_DEBUG == -2:
                s.emit("vector", lambda e, h=h: e.tensor_mul(
                    Pm[h][:], psS[:, h * 128:(h + 1) * 128], mUi[:]))
            if B1_DEBUG >= 3:
                s.emit("scalar", lambda e, h=h: e.copy(ktm[h][:], ptK[:, h * 64:(h + 1) * 64]))
        s.barrier()
        for h in range(4):
            ti, po = hs(h)
            s.emit("tensor", lambda e, h=h, va=vap[h]: e.matmul(
                psO[:, h * 64:(h + 1) * 64], Pm[h][:], va,
                start=True, stop=False))
            s.emit("tensor", lambda e, h=h, ti=ti, po=po: e.matmul(
                psO[:, h * 64:(h + 1) * 64], qt[ti][po:po + 64, :],
                S0[h][po:po + 64, :], start=False, stop=True))
        s.barrier()
        for h in range(4):
            ti, po = hs(h)
            s.emit("vector", lambda e, h=h: e.tensor_copy(
                osb[h][:], psO[:, h * 64:(h + 1) * 64]))
            s.emit("tensor", lambda e, h=h, po=po, va=vap[h]: e.matmul(
                psT[po:po + 64, h * 64:(h + 1) * 64], ktm[h][:], va,
                start=True, stop=False))
            s.emit("tensor", lambda e, h=h, po=po: e.matmul(
                psT[po:po + 64, h * 64:(h + 1) * 64],
                idn[po:po + 64, po:po + 64], S0[h][po:po + 64, :],
                start=False, stop=True))
        s.barrier()
        for h in range(4):
            ti, po = hs(h)
            s.emit("vector", lambda e, h=h, ti=ti, po=po: e.tensor_scalar(
                S0[h][po:po + 64, :], psT[po:po + 64, h * 64:(h + 1) * 64],
                Pq[ti][po:po + 64, CS - 1:CS], None, ALU.mult))
            s.dma("sync", o[t0:t0 + CS, h * 64:(h + 1) * 64], osb[h][:])
        s.barrier()
    s.build(block, sem)
    es.close()
    return nc



_BUILD_CACHE = {}
DEBUG_STAGE_HOOK = None


def _dbg(tag, xcm):
    if DEBUG_STAGE_HOOK is not None:
        DEBUG_STAGE_HOOK(tag, xcm)


def _get(name, fn, *args):
    key = (name,) + tuple(args)
    if key not in _BUILD_CACHE:
        _BUILD_CACHE[key] = fn(*args)
    return _BUILD_CACHE[key]


def _run(nc, in_maps):
    res = run_bass_kernel_spmd(nc, in_maps, core_ids=list(range(NCORES)))
    return res.results


def _cm(a):
    return np.ascontiguousarray(a.T)


def _colform(v, nblk):
    return np.ascontiguousarray(np.asarray(v, np.float32).reshape(nblk, 128).T)


def _causal_masks():
    kk = np.arange(128)[:, None]
    qq = np.arange(512)[None, :]
    out = np.zeros((128, 4 * 512), np.float32)
    for mi in range(4):
        r = mi * 128
        out[:, mi * 512:(mi + 1) * 512] = (qq - kk - r >= 0).astype(np.float32)
    return out


def _window_masks(W=256):
    kk = np.arange(128)[:, None]
    qq = np.arange(512)[None, :]
    out = np.zeros((128, 6 * 512), np.float32)
    for mi in range(6):
        r = mi * 128 - 256
        d = qq - kk - r
        out[:, mi * 512:(mi + 1) * 512] = ((d >= 0) & (d < W)).astype(np.float32)
    return out


def _rope_tables(pos, base, nh):
    inv = (1.0 / base ** (np.arange(0, HD, 2, dtype=np.float32) / HD)).astype(np.float32)
    ang = pos.astype(np.float32)[:, None] * inv[None, :]
    cos, sin = np.cos(ang).astype(np.float32), np.sin(ang).astype(np.float32)
    ntt = len(pos) // 128

    def strip(t):
        t3 = t.reshape(ntt, 128, 32)
        t4 = np.repeat(t3[:, :, None, :], nh, axis=2)
        return np.ascontiguousarray(t4.transpose(1, 0, 2, 3).reshape(128, ntt * nh * 32))
    return strip(cos), strip(sin)


def _vstrip(v_tm, append_ones=True):
    seq = v_tm.shape[0]
    nt = seq // 128
    t3 = np.asarray(v_tm).reshape(nt, 128, 64)
    if append_ones:
        ones = np.ones((nt, 128, 1), dtype=v_tm.dtype)
        t3 = np.concatenate([t3, ones], axis=2)
    w = t3.shape[2]
    return np.ascontiguousarray(t3.transpose(1, 0, 2).reshape(128, nt * w))


def _gather_heads_cm(tm_list, b, heads, width=64):
    full = np.concatenate([tm_list[4 * b + j] for j in range(4)], axis=0)
    rows = [np.ascontiguousarray(full[:, h * width:(h + 1) * width].T) for h in heads]
    return np.ascontiguousarray(np.vstack(rows))


def _heads_to_tokcm(o_list, c):
    b = c // 4
    full = np.concatenate([o_list[4 * b + j] for j in range(4)], axis=1)
    sl = full[(c % 4) * TOK:((c % 4) + 1) * TOK, :]
    return np.ascontiguousarray(sl.T)


def _run_ffn(xcm, wd):
    nc = _get("ffn", build_ffn)
    maps = [dict(x=xcm[c], **wd) for c in range(NCORES)]
    res = _run(nc, maps)
    return [res[c]["xn"] for c in range(NCORES)]


def _run_tok_post(xcm, o_cm, w):
    nc = _get("post", build_tok_post)
    maps = [dict(o=o_cm[c], w=w, x=xcm[c]) for c in range(NCORES)]
    res = _run(nc, maps)
    return [res[c]["xn"] for c in range(NCORES)]


def kernel(**inp):
    f32 = np.float32
    x = np.ascontiguousarray(inp["L0_attn_a_x"].astype(f32).reshape(16384, D))
    xcm = [np.ascontiguousarray(x[c * TOK:(c + 1) * TOK].T) for c in range(NCORES)]
    cmasks = _causal_masks()
    wmasks = _window_masks()
    iden = np.eye(128, dtype=f32)
    mlow = np.tril(np.ones((128, 128), f32), -1)
    mup = np.triu(np.ones((128, 128), f32), 1)
    muppi = np.triu(np.ones((128, 128), f32), 0)

    nc = _get("pre24", build_tok_pre, 24)
    wqkvT = np.ascontiguousarray(inp["L0_attn_a_w_qkv"].T.astype(f32))
    res = _run(nc, [dict(x=xcm[c], w=wqkvT) for c in range(NCORES)])
    qkv = [res[c]["out"] for c in range(NCORES)]

    nc = _get("attn_a0", build_attn_a0)
    maps = []
    for hc in range(NCORES):
        b = hc // 4
        heads = [4 * (hc % 4) + j for j in range(4)]
        qkv_b = np.concatenate([qkv[4 * b + j] for j in range(4)], axis=1)
        qin = np.ascontiguousarray(np.vstack(
            [qkv_b[h * 64:(h + 1) * 64] for h in heads]))
        kin = np.ascontiguousarray(np.vstack(
            [qkv_b[1024 + h * 64:1024 + (h + 1) * 64] for h in heads]))
        vstrips = [_vstrip(np.ascontiguousarray(qkv_b[2048 + h * 64:2048 + (h + 1) * 64].T))
                   for h in heads]
        vin = np.ascontiguousarray(np.concatenate(vstrips, axis=1))
        maps.append(dict(q=qin, k=kin, v=vin, cm=cmasks))
    res = _run(nc, maps)
    o_a0 = [res[hc]["o"] for hc in range(NCORES)]
    o_cm = [_heads_to_tokcm(o_a0, c) for c in range(NCORES)]
    wprojT = np.ascontiguousarray(inp["L0_attn_a_w_proj"].T.astype(f32))
    xcm = _run_tok_post(xcm, o_cm, wprojT)
    _dbg("s1", xcm)

    ffn0 = dict(
        w1t=np.ascontiguousarray(inp["L0_ffn_w1_w"].T.astype(f32)),
        w2t=np.ascontiguousarray(inp["L0_ffn_w2_w"].T.astype(f32)),
        h1t=np.ascontiguousarray(inp["L0_ffn_h1_w"].T.astype(f32)),
        h2t=np.ascontiguousarray(inp["L0_ffn_h2_w"].T.astype(f32)),
        w3t=np.ascontiguousarray(inp["L0_ffn_w3_w"].T.astype(f32)),
        s1=_colform(inp["L0_ffn_w1_s"], 32), s2=_colform(inp["L0_ffn_w2_s"], 32),
        sh1=_colform(inp["L0_ffn_h1_s"], 32), sh2=_colform(inp["L0_ffn_h2_s"], 32),
        s3=_colform(inp["L0_ffn_w3_s"], 8))
    xcm = _run_ffn(xcm, ffn0)
    _dbg("s2", xcm)

    nc = _get("b0n", build_b0_norms)
    g1 = _colform(inp["L0_attn_b_norm1_w"], NB)
    res = _run(nc, [dict(x=xcm[c], g1=g1) for c in range(NCORES)])
    n_f32 = [res[c]["n_out"] for c in range(NCORES)]
    m1 = [res[c]["m_out"] for c in range(NCORES)]

    nc = _get("b0qkv_g", build_b0_qkv, D, True)
    kvup = np.ascontiguousarray(np.concatenate(
        [inp["L0_attn_b_g_k_up"], inp["L0_attn_b_g_v_up"]], axis=1).astype(f32))
    maps = []
    for c in range(NCORES):
        pos = (c % 4) * TOK + np.arange(TOK)
        cq, sq = _rope_tables(pos, 1e6, H)
        ck, sk = _rope_tables(pos, 1e6, 4)
        maps.append(dict(m=m1[c], wq=inp["L0_attn_b_g_wq"].astype(f32),
                         gkvd=inp["L0_attn_b_g_kv_down"].astype(f32), kvup=kvup,
                         cosq=cq, sinq=sq, cosk=ck, sink=sk))
    res = _run(nc, maps)
    qg = [res[c]["q_out"] for c in range(NCORES)]
    kg = [res[c]["k_out"] for c in range(NCORES)]
    vg = [res[c]["v_out"] for c in range(NCORES)]

    nc = _get("attn_b0g", build_attn_b0, None)
    maps = []
    for hc in range(NCORES):
        b = hc // 4
        heads = [4 * (hc % 4) + j for j in range(4)]
        kvh = hc % 4
        qin = _gather_heads_cm(qg, b, heads)
        kcm1 = _gather_heads_cm(kg, b, [kvh])
        kin = np.ascontiguousarray(np.vstack([kcm1, kcm1]))
        v_full = np.concatenate([vg[4 * b + j] for j in range(4)], axis=0)
        vin = _vstrip(np.ascontiguousarray(v_full[:, kvh * 64:(kvh + 1) * 64]))
        maps.append(dict(q=qin, k=kin, v=vin, mk=cmasks))
    res = _run(nc, maps)
    og = [res[hc]["o"] for hc in range(NCORES)]
    og_cm = [_heads_to_tokcm(og, c) for c in range(NCORES)]

    nc = _get("b0mid", build_b0_mid)
    g2 = _colform(inp["L0_attn_b_norm2_w"], NB)
    res = _run(nc, [dict(og=og_cm[c], gwo=inp["L0_attn_b_g_wo"].astype(f32),
                         n_in=n_f32[c], x=xcm[c], g2=g2) for c in range(NCORES)])
    xp = [res[c]["xp_out"] for c in range(NCORES)]
    m2 = [res[c]["m_out"] for c in range(NCORES)]

    nc = _get("b0qkv_l", build_b0_qkv, D, False)
    wkv = np.ascontiguousarray(np.concatenate(
        [inp["L0_attn_b_l_wk"], inp["L0_attn_b_l_wv"]], axis=1).astype(f32))
    maps = []
    for c in range(NCORES):
        pos = (c % 4) * TOK + np.arange(TOK)
        cq, sq = _rope_tables(pos, 1e4, H)
        ck, sk = _rope_tables(pos, 1e4, 4)
        maps.append(dict(m=m2[c], wq=inp["L0_attn_b_l_wq"].astype(f32), wkv=wkv,
                         cosq=cq, sinq=sq, cosk=ck, sink=sk))
    res = _run(nc, maps)
    ql = [res[c]["q_out"] for c in range(NCORES)]
    kl = [res[c]["k_out"] for c in range(NCORES)]
    vl = [res[c]["v_out"] for c in range(NCORES)]

    nc = _get("attn_b0l", build_attn_b0, 256)
    maps = []
    for hc in range(NCORES):
        b = hc // 4
        heads = [4 * (hc % 4) + j for j in range(4)]
        kvh = hc % 4
        qin = _gather_heads_cm(ql, b, heads)
        kcm1 = _gather_heads_cm(kl, b, [kvh])
        kin = np.ascontiguousarray(np.vstack([kcm1, kcm1]))
        v_full = np.concatenate([vl[4 * b + j] for j in range(4)], axis=0)
        vin = _vstrip(np.ascontiguousarray(v_full[:, kvh * 64:(kvh + 1) * 64]))
        maps.append(dict(q=qin, k=kin, v=vin, mk=wmasks))
    res = _run(nc, maps)
    ol = [res[hc]["o"] for hc in range(NCORES)]
    ol_cm = [_heads_to_tokcm(ol, c) for c in range(NCORES)]
    xcm = _run_tok_post(xp, ol_cm, inp["L0_attn_b_l_wo"].astype(f32))
    _dbg("s3", xcm)

    xcm = _run_ffn(xcm, ffn0)
    _dbg("s4", xcm)

    nc = _get("a1pre", build_a1_pre)
    res = _run(nc, [dict(x=xcm[c], wq=inp["L1_attn_a_Wq"].astype(f32),
                         wk=inp["L1_attn_a_Wk"].astype(f32),
                         wv=inp["L1_attn_a_Wv"].astype(f32)) for c in range(NCORES)])
    qa = [res[c]["q_out"] for c in range(NCORES)]
    ka = [res[c]["k_out"] for c in range(NCORES)]
    va = [res[c]["v_out"] for c in range(NCORES)]

    nc = _get("a1scan", build_a1_scan)
    ar_all = inp["L1_attn_a_alpha_raw"].astype(f32).reshape(H)
    br_all = inp["L1_attn_a_beta_raw"].astype(f32).reshape(H)
    maps = []
    for hc in range(NCORES):
        b = hc // 4
        heads = [4 * (hc % 4) + j for j in range(4)]
        araw = np.ascontiguousarray(np.tile(ar_all[heads].reshape(1, 4), (128, 1)))
        braw = np.ascontiguousarray(np.tile(br_all[heads].reshape(1, 4), (128, 1)))
        qin = _gather_heads_cm(qa, b, heads)
        kin = _gather_heads_cm(ka, b, heads)
        k_full = np.concatenate([ka[4 * b + j] for j in range(4)], axis=0)
        v_full = np.concatenate([va[4 * b + j] for j in range(4)], axis=0)
        ktm = np.concatenate([_vstrip(np.ascontiguousarray(
            k_full[:, h * 64:(h + 1) * 64]), append_ones=False) for h in heads], axis=1)
        vtm = np.concatenate([_vstrip(np.ascontiguousarray(
            v_full[:, h * 64:(h + 1) * 64]), append_ones=False) for h in heads], axis=1)
        maps.append(dict(q=qin, k=kin, ktm=np.ascontiguousarray(ktm),
                         vtm=np.ascontiguousarray(vtm), araw=araw, braw=braw,
                         mlow=mlow, mup=mup, iden=iden))
    res = _run(nc, maps)
    ya = [res[hc]["y"] for hc in range(NCORES)]
    ya_cm = [_heads_to_tokcm(ya, c) for c in range(NCORES)]
    xcm = _run_tok_post(xcm, ya_cm, inp["L1_attn_a_Wo"].astype(f32))
    _dbg("s5", xcm)

    ffn1 = dict(
        w1t=np.ascontiguousarray(inp["L1_ffn_w1_w"].T.astype(f32)),
        w2t=np.ascontiguousarray(inp["L1_ffn_w2_w"].T.astype(f32)),
        h1t=np.ascontiguousarray(inp["L1_ffn_h1_w"].T.astype(f32)),
        h2t=np.ascontiguousarray(inp["L1_ffn_h2_w"].T.astype(f32)),
        w3t=np.ascontiguousarray(inp["L1_ffn_w3_w"].T.astype(f32)),
        s1=_colform(inp["L1_ffn_w1_s"], 32), s2=_colform(inp["L1_ffn_w2_s"], 32),
        sh1=_colform(inp["L1_ffn_h1_s"], 32), sh2=_colform(inp["L1_ffn_h2_s"], 32),
        s3=_colform(inp["L1_ffn_w3_s"], 8))
    xcm = _run_ffn(xcm, ffn1)
    _dbg("s6", xcm)

    nc = _get("b1pre", build_b1_pre)
    res = _run(nc, [dict(x=xcm[c], wq=inp["L1_attn_b_W_q"].astype(f32),
                         wk=inp["L1_attn_b_W_k"].astype(f32),
                         wv=inp["L1_attn_b_W_v"].astype(f32),
                         wg=inp["L1_attn_b_W_gate"].astype(f32),
                         bg=_colform(inp["L1_attn_b_b_gate"], NB))
                    for c in range(NCORES)])
    qb = [res[c]["q_out"] for c in range(NCORES)]
    kb = [res[c]["k_out"] for c in range(NCORES)]
    vb = [res[c]["v_out"] for c in range(NCORES)]
    lgb = [res[c]["lg_out"] for c in range(NCORES)]

    nc = _get("b1scan", build_b1_scan)
    maps = []
    for hc in range(NCORES):
        b = hc // 4
        heads = [4 * (hc % 4) + j for j in range(4)]
        q_full = np.concatenate([qb[4 * b + j] for j in range(4)], axis=1)
        k_full = np.concatenate([kb[4 * b + j] for j in range(4)], axis=1)
        lg_full = np.concatenate([lgb[4 * b + j] for j in range(4)], axis=1)
        qin = np.ascontiguousarray(np.vstack(
            [q_full[h * 64:(h + 1) * 64] for h in heads]))
        kin = np.ascontiguousarray(np.vstack(
            [k_full[h * 64:(h + 1) * 64] for h in heads]))
        lgin = np.ascontiguousarray(np.vstack(
            [lg_full[h * 64:(h + 1) * 64] for h in heads]))
        v_full = np.concatenate([vb[4 * b + j] for j in range(4)], axis=0)
        vtm = np.concatenate([_vstrip(np.ascontiguousarray(
            v_full[:, h * 64:(h + 1) * 64]), append_ones=False) for h in heads], axis=1)
        maps.append(dict(q=qin, k=kin, vtm=np.ascontiguousarray(vtm), lg=lgin,
                         muppi=muppi, iden=iden))
    res = _run(nc, maps)
    ob = [res[hc]["o"] for hc in range(NCORES)]
    ob_cm = [_heads_to_tokcm(ob, c) for c in range(NCORES)]
    xcm = _run_tok_post(xcm, ob_cm, inp["L1_attn_b_W_o"].astype(f32))
    _dbg("s7", xcm)

    xcm = _run_ffn(xcm, ffn1)
    _dbg("s8", xcm)

    out = np.empty((16384, D), np.float32)
    for c in range(NCORES):
        out[c * TOK:(c + 1) * TOK] = np.asarray(xcm[c], np.float32).T
    return out.reshape(B, S, D)
